# version 02
# baseline (speedup 1.0000x reference)
import math
import numpy as np
import ml_dtypes
import concourse.bass as bass
import concourse.mybir as mybir
from concourse.bass_utils import run_bass_kernel_spmd
from contextlib import ExitStack

F32 = mybir.dt.float32
BF16 = mybir.dt.bfloat16
AF = mybir.ActivationFunctionType
ALU = mybir.AluOpType
AX = mybir.AxisListType.X

NCORES = 8
D = 2048
T = 8192
TC = T // NCORES
NT = TC // 128
NDM = D // 128
NH = 16
HD = 128
SCL = 1.0 / math.sqrt(HD)
DIN = 4096
NHM = 64
HPM = 64
DS = 128
DCONV = 4
CH = 256
INTER = 8192
EPS = 1e-5
HPC = NHM // NCORES

_bf = ml_dtypes.bfloat16



def _ln_pre(v, x_ap, u_ap, ntile, width, raws, m_cols, v_cols, s2_cols):
    inv_n = 1.0 / width
    for t in range(ntile):
        xs = x_ap[:, t * width:(t + 1) * width]
        v.reduce_sum(raws[:, t:t + 1], xs, axis=AX)
        v.scalar_tensor_tensor(out=u_ap[:, t * width:(t + 1) * width],
                               in0=xs, scalar=1.0, in1=xs,
                               op0=ALU.mult, op1=ALU.mult,
                               accum_out=v_cols[:, t:t + 1])
    v.drain()
    for t in range(ntile):
        v.tensor_scalar_mul(m_cols[:, t:t + 1], raws[:, t:t + 1], inv_n)
        v.scalar_tensor_tensor(out=s2_cols[:, t:t + 1], in0=raws[:, t:t + 1],
                               scalar=inv_n * inv_n, in1=raws[:, t:t + 1],
                               op0=ALU.mult, op1=ALU.mult)
        v.tensor_scalar_mul(v_cols[:, t:t + 1], v_cols[:, t:t + 1], inv_n)
    v.drain()
    for t in range(ntile):
        v.tensor_sub(v_cols[:, t:t + 1], v_cols[:, t:t + 1],
                     s2_cols[:, t:t + 1])


def _ln_mid(s, ntile, v_cols, r_cols, eps_col):
    for t in range(ntile):
        s.activation(r_cols[:, t:t + 1], v_cols[:, t:t + 1], AF.Ln,
                     bias=eps_col, scale=1.0)
    s.drain()
    for t in range(ntile):
        s.activation(r_cols[:, t:t + 1], r_cols[:, t:t + 1], AF.Exp,
                     scale=-0.5)


def _ln_post(v, x_ap, u_ap, ntile, width, m_cols, r_cols):
    for t in range(ntile):
        v.tensor_scalar(out=u_ap[:, t * width:(t + 1) * width],
                        in0=x_ap[:, t * width:(t + 1) * width],
                        scalar1=m_cols[:, t:t + 1], scalar2=r_cols[:, t:t + 1],
                        op0=ALU.subtract, op1=ALU.mult)


def _rope_from(v, src_f32, dst_bf, cos_lo, sin_lo, cos_hi, sin_hi,
               ta, tb, tc_, td, thi, ncols):
    v.tensor_copy(thi, src_f32[32:64, 0:ncols])
    v.tensor_copy(dst_bf[64:128, 0:ncols], src_f32[64:128, 0:ncols])
    v.drain()
    v.tensor_mul(ta, src_f32[0:32, 0:ncols], cos_lo)
    v.tensor_mul(tb, thi, sin_lo)
    v.tensor_mul(tc_, thi, cos_hi)
    v.tensor_mul(td, src_f32[0:32, 0:ncols], sin_hi)
    v.drain()
    v.tensor_sub(dst_bf[0:32, 0:ncols], ta, tb)
    v.tensor_add(dst_bf[32:64, 0:ncols], tc_, td)



def build_A():
    nc = bass.Bass("TRN2", target_bir_lowering=False)
    xs_d = nc.dram_tensor("xs", [TC, D], F32, kind="ExternalInput").ap()
    wkT_d = nc.dram_tensor("wkT", [D, D], F32, kind="ExternalInput").ap()
    wvT_d = nc.dram_tensor("wvT", [D, D], F32, kind="ExternalInput").ap()
    cos_d = nc.dram_tensor("cosT", [64, TC], F32, kind="ExternalInput").ap()
    sin_d = nc.dram_tensor("sinT", [64, TC], F32, kind="ExternalInput").ap()
    id_d = nc.dram_tensor("identin", [128, 128], BF16, kind="ExternalInput").ap()
    kt_d = nc.dram_tensor("kt_out", [D, TC], BF16, kind="ExternalOutput").ap()
    vt_d = nc.dram_tensor("vt_out", [D, TC], BF16, kind="ExternalOutput").ap()
    kn_d = nc.dram_tensor("kn_out", [1, NH], F32, kind="ExternalOutput").ap()

    NR = 2 * NDM

    with ExitStack() as ctx:
        sb = lambda n, s, d=F32: ctx.enter_context(nc.sbuf_tensor(n, s, d))
        ps = lambda n, s, d=F32: ctx.enter_context(nc.psum_tensor(n, s, d))
        sem = lambda n: ctx.enter_context(nc.semaphore(n))

        x_sb = sb("x_sb", [128, NT * D])
        u_sb = sb("u_sb", [128, NT * D], BF16)
        uT = sb("uT", [128, NDM * TC], BF16)
        ident = sb("ident", [128, 128], BF16)
        cos_sb = sb("cos_sb", [64, TC])
        sin_sb = sb("sin_sb", [64, TC])
        cos_hi = sb("cos_hi", [32, TC])
        sin_hi = sb("sin_hi", [32, TC])
        w_blk = [sb(f"w_blk{i}", [128, NDM * 128], BF16) for i in range(2)]
        kt_sb = [sb(f"kt_sb{i}", [128, TC], BF16) for i in range(2)]
        kn_sb = sb("kn_sb", [1, NH])
        ones_col = sb("ones_col", [128, 1], BF16)
        raws = sb("raws", [128, NT])
        m_cols = sb("m_cols", [128, NT])
        v_cols = sb("v_cols", [128, NT])
        r_cols = sb("r_cols", [128, NT])
        s2_cols = sb("s2_cols", [128, NT])
        eps_col = sb("eps_col", [128, 1])
        tmp_f = sb("tmp_f", [128, TC])
        ta = sb("ta", [32, TC])
        tb = sb("tb", [32, TC])
        tc_ = sb("tc_", [32, TC])
        td = sb("td", [32, TC])
        thi = sb("thi", [32, TC])
        mabs = sb("mabs", [128, NH])
        msq = sb("msq", [128, NH], BF16)

        pp = [ps(f"pp{i}", [128, 512]) for i in range(4)]
        tr_ps = ps("tr_ps", [128, 512], BF16)
        nrm_ps = ps("nrm_ps", [1, NH])

        ld = sem("ld"); wl = [sem("wl0"), sem("wl1")]; wf = sem("wf")
        ot = [sem("ot0"), sem("ot1")]
        lns = sem("lns"); trs = sem("trs"); tre = sem("tre")
        prj = sem("prj"); evc = sem("evc"); rp = sem("rp")
        msq_s = sem("msq_s"); nm = sem("nm"); knc = sem("knc")
        out_s = sem("out_s")

        block = ctx.enter_context(nc.Block())
        n_tr = NDM * NT // 4

        @block.sync
        def _(sync):
            sync.dma_start(out=x_sb[:].rearrange("p (t d) -> p t d", t=NT),
                           in_=xs_d.rearrange("(t p) d -> p t d", p=128)
                           ).then_inc(ld, 16)
            sync.dma_start(out=cos_sb[:], in_=cos_d).then_inc(ld, 16)
            sync.dma_start(out=sin_sb[:], in_=sin_d).then_inc(ld, 16)
            sync.dma_start(out=ident[:], in_=id_d).then_inc(ld, 16)
            for r in range(NR):
                sync.wait_ge(rp, r + 1)
                dmt = r % NDM
                dst = kt_d if r < NDM else vt_d
                sync.dma_start(out=dst[dmt * 128:(dmt + 1) * 128, :],
                               in_=kt_sb[r % 2][:]).then_inc(ot[r % 2], 16)
            sync.wait_ge(knc, 1)
            sync.dma_start(out=kn_d, in_=kn_sb[0:1, 0:NH]).then_inc(ld, 16)
            sync.wait_ge(ot[0], 16 * (NR // 2))
            sync.wait_ge(ot[1], 16 * (NR // 2))
            sync.wait_ge(ld, 80)

        @block.gpsimd
        def _(g):
            for r in range(NR):
                w_d = wkT_d if r < NDM else wvT_d
                dmt = r % NDM
                if r >= 2:
                    g.wait_ge(wf, r - 1)
                g.dma_start(
                    out=w_blk[r % 2][:].rearrange("p (k c) -> p k c", k=NDM),
                    in_=w_d[:, dmt * 128:(dmt + 1) * 128]
                        .rearrange("(k p) c -> p k c", p=128),
                ).then_inc(wl[r % 2], 16)

        @block.vector
        def _(v):
            v.memset(ones_col[:], 1.0)
            v.memset(eps_col[:], EPS)
            v.wait_ge(ld, 64)
            v.tensor_copy(cos_hi[:], cos_sb[32:64, :])
            v.tensor_copy(sin_hi[:], sin_sb[32:64, :])
            v.drain()
            _ln_pre(v, x_sb, u_sb, NT, D, raws, m_cols, v_cols, s2_cols)
            v.drain().then_inc(lns, 1)

        @block.scalar
        def _(s):
            s.wait_ge(lns, 1)
            _ln_mid(s, NT, v_cols, r_cols, eps_col[:])
            s.drain().then_inc(lns, 1)

        @block.vector
        def _(v):
            v.wait_ge(lns, 2)
            _ln_post(v, x_sb, u_sb, NT, D, m_cols, r_cols)
            v.drain().then_inc(lns, 1)

        @block.tensor
        def _(t):
            t.wait_ge(ld, 64)
            t.wait_ge(lns, 3)
            for r in range(n_tr):
                dmt, tt4 = divmod(r, NT // 4)
                if r >= 1:
                    t.wait_ge(tre, r)
                for j in range(4):
                    tt = tt4 * 4 + j
                    t.transpose(tr_ps[:, j * 128:(j + 1) * 128],
                                u_sb[:, tt * D + dmt * 128:
                                     tt * D + (dmt + 1) * 128],
                                ident[:])
                t.drain().then_inc(trs, 1)
            t.wait_ge(tre, n_tr)
            for r in range(NR):
                wb = w_blk[r % 2]
                t.wait_ge(wl[r % 2], 16 * (r // 2 + 1))
                if r >= 2:
                    t.wait_ge(evc, r - 1)
                for h in range(2):
                    for k in range(NDM):
                        t.matmul(pp[2 * (r % 2) + h][:],
                                 wb[:, k * 128:(k + 1) * 128],
                                 uT[:, k * TC + h * 512:k * TC + h * 512 + 512],
                                 start=(k == 0), stop=(k == NDM - 1))
                t.drain().then_inc(prj, 1)
                t.drain().then_inc(wf, 1)
            t.wait_ge(msq_s, 1)
            for h in range(NH):
                t.matmul(nrm_ps[0:1, h:h + 1], ones_col[:],
                         msq[:, h:h + 1], start=True, stop=True)
            t.drain().then_inc(nm, 1)

        @block.vector
        def _(v):
            for r in range(n_tr):
                v.wait_ge(trs, r + 1)
                dmt, tt4 = divmod(r, NT // 4)
                v.tensor_copy(
                    uT[:, dmt * TC + tt4 * 512:dmt * TC + (tt4 + 1) * 512],
                    tr_ps[:]).then_inc(tre, 1)
            for r in range(NR):
                v.wait_ge(prj, r + 1)
                dst = kt_sb[r % 2]
                if r >= 2:
                    v.wait_ge(ot[r % 2], 16 * (r // 2))
                if r < NDM:
                    v.tensor_copy(tmp_f[:, 0:512], pp[2 * (r % 2)][:])
                    v.tensor_copy(tmp_f[:, 512:1024], pp[2 * (r % 2) + 1][:])
                    v.drain().then_inc(evc, 1)
                    v.reduce_max(mabs[:, r:r + 1], tmp_f[:], axis=AX,
                                 apply_absolute_value=True)
                    _rope_from(v, tmp_f, dst, cos_sb[0:32, :], sin_sb[0:32, :], cos_hi[:], sin_hi[:], ta[:], tb[:], tc_[:], td[:], thi[:], TC)
                    v.drain().then_inc(rp, 1)
                else:
                    v.tensor_copy(dst[:, 0:512], pp[2 * (r % 2)][:])
                    v.tensor_copy(dst[:, 512:1024], pp[2 * (r % 2) + 1][:])
                    v.drain().then_inc(evc, 1)
                    v.drain().then_inc(rp, 1)
            v.drain()
            v.scalar_tensor_tensor(out=msq[:], in0=mabs[:], scalar=1.0,
                                   in1=mabs[:], op0=ALU.mult, op1=ALU.mult)
            v.drain().then_inc(msq_s, 1)
            v.wait_ge(nm, 1)
            v.tensor_copy(kn_sb[0:1, 0:NH], nrm_ps[0:1, 0:NH])
            v.drain().then_inc(knc, 1)

    return nc



def ffn_gpsimd(g, cfg):
    gl, gf, dl, dad = cfg["gl"], cfg["gf"], cfg["dl"], cfg["dad"]
    b = cfg
    ss = 0
    dr = 0
    for ib in range(4):
        for j in range(16):
            jg = ib * 16 + j
            for col in (jg, jg + 64):
                slot = ss % 2
                if ss >= 2:
                    g.wait_ge(gf, b["gf0"] + ss - 1)
                g.dma_start(
                    out=cfg["wcol"][slot].rearrange("p (k c) -> p k c", k=NDM),
                    in_=cfg["wguT_d"][:, col * 128:(col + 1) * 128]
                        .rearrange("(k p) c -> p k c", p=128),
                ).then_inc(gl[slot], 16)
                ss += 1
        for sub in range(2):
            i0 = ib * 2048 + sub * 1024
            if dr >= 1:
                g.wait_ge(cfg["dpr"], b["dpr0"] + 8 * dr)
            g.dma_start(
                out=cfg["wbig"].rearrange("p (k c) -> p k c", k=8),
                in_=cfg["wdT_d"][i0:i0 + 1024, :]
                    .rearrange("(k p) c -> p k c", p=128),
            ).then_inc(dl[dr % 2], 16)
            dr += 1


def ffn_tensor(t, cfg):
    uT, hT, wbig = cfg["uT"], cfg["hT"], cfg["wbig"]
    A = cfg["A"]
    gl, gf, fpr, fsi, fev = (cfg["gl"], cfg["gf"], cfg["fpr"], cfg["fsi"],
                             cfg["fev"])
    dl, dpr, dad = cfg["dl"], cfg["dpr"], cfg["dad"]
    b = cfg
    ss = 0
    dr = 0
    for ib in range(4):
        if ib >= 1:
            t.wait_ge(dad, b["dad0"] + 16 * ib)
        for j in range(16):
            jp = ib * 16 + j
            for k in range(2):
                slot = ss % 2
                t.wait_ge(gl[slot], 16 * (ss // 2 + 1) + b["gl0"][slot])
                if k == 0 and jp >= 1:
                    t.wait_ge(fsi, b["fsi0"] + jp)
                if k == 1 and jp >= 1:
                    t.wait_ge(fev, b["fev0"] + jp)
                for kk in range(NDM):
                    for hh in range(2):
                        t.matmul(A[2 * k + hh][:],
                                 cfg["wcol"][slot][:, kk * 128:(kk + 1) * 128],
                                 uT[:, kk * TC + hh * 512:kk * TC + hh * 512 + 512],
                                 start=(kk == 0), stop=(kk == NDM - 1))
                t.drain().then_inc(fpr, 1)
                t.drain().then_inc(gf, 1)
                ss += 1
        for sub in range(2):
            t.wait_ge(dl[dr % 2], 16 * (dr // 2 + 1) + b["dl0"][dr % 2])
            t.wait_ge(fsi, b["fsi0"] + 16 * (ib + 1))
            t.wait_ge(fev, b["fev0"] + 16 * (ib + 1))
            for ts in range(8):
                gidx = dr * 8 + ts
                if gidx >= 1:
                    t.wait_ge(dad, b["dad0"] + gidx)
                for it in range(8):
                    for n in range(4):
                        t.matmul(A[n][:],
                                 hT[:, (sub * 8 + it) * TC + ts * 128:
                                    (sub * 8 + it) * TC + ts * 128 + 128],
                                 wbig[:, it * D + n * 512:it * D + n * 512 + 512],
                                 start=(it == 0), stop=(it == 7))
                t.drain().then_inc(dpr, 1)
            dr += 1


def ffn_scalar(s, cfg):
    A, hg = cfg["A"], cfg["hg"]
    fpr, fsi, fev = cfg["fpr"], cfg["fsi"], cfg["fev"]
    b = cfg
    for jp in range(64):
        s.wait_ge(fpr, b["fpr0"] + 2 * jp + 1)
        if jp >= 2:
            s.wait_ge(fev, b["fev0"] + jp - 1)
        sl = (jp % 2) * TC
        s.activation(hg[:, sl:sl + 512], A[0][:], AF.Silu)
        s.activation(hg[:, sl + 512:sl + 1024], A[1][:], AF.Silu)
        s.drain().then_inc(fsi, 1)


def ffn_vector(v, cfg):
    A, hg, hT, x_sb = cfg["A"], cfg["hg"], cfg["hT"], cfg["x_sb"]
    fpr, fsi, fev = cfg["fpr"], cfg["fsi"], cfg["fev"]
    dpr, dad = cfg["dpr"], cfg["dad"]
    b = cfg
    for ib in range(4):
        for jpl in range(16):
            jp = ib * 16 + jpl
            v.wait_ge(fsi, b["fsi0"] + jp + 1)
            v.wait_ge(fpr, b["fpr0"] + 2 * jp + 2)
            if jpl == 0 and ib >= 1:
                v.wait_ge(dpr, b["dpr0"] + 16 * ib)
            sl = (jp % 2) * TC
            v.tensor_mul(hT[:, jpl * TC:jpl * TC + 512], hg[:, sl:sl + 512],
                         A[2][:])
            v.tensor_mul(hT[:, jpl * TC + 512:jpl * TC + 1024],
                         hg[:, sl + 512:sl + 1024], A[3][:])
            v.drain().then_inc(fev, 1)
        for sub in range(2):
            for ts in range(8):
                v.wait_ge(dpr, b["dpr0"] + ib * 16 + sub * 8 + ts + 1)
                for n in range(4):
                    v.tensor_add(x_sb[:, ts * D + n * 512:ts * D + n * 512 + 512],
                                 x_sb[:, ts * D + n * 512:ts * D + n * 512 + 512],
                                 A[n][:])
                v.drain().then_inc(dad, 1)



def build_B():
    nc = bass.Bass("TRN2", target_bir_lowering=False)
    xs_d = nc.dram_tensor("xs", [TC, D], F32, kind="ExternalInput").ap()
    ktf_d = nc.dram_tensor("ktf", [D, T], BF16, kind="ExternalInput").ap()
    vh_d = nc.dram_tensor("vhost", [NH, 128, T], BF16, kind="ExternalInput").ap()
    kn_d = nc.dram_tensor("kn8row", [1, NCORES * NH], F32, kind="ExternalInput").ap()
    wqT_d = nc.dram_tensor("wqT", [D, D], F32, kind="ExternalInput").ap()
    woT_d = nc.dram_tensor("woT", [D, D], F32, kind="ExternalInput").ap()
    wguT_d = nc.dram_tensor("wguT", [D, 2 * INTER], F32, kind="ExternalInput").ap()
    wdT_d = nc.dram_tensor("wdT", [INTER, D], F32, kind="ExternalInput").ap()
    cos_d = nc.dram_tensor("cosT", [64, TC], F32, kind="ExternalInput").ap()
    sin_d = nc.dram_tensor("sinT", [64, TC], F32, kind="ExternalInput").ap()
    id_d = nc.dram_tensor("identin", [128, 128], BF16, kind="ExternalInput").ap()
    x2_d = nc.dram_tensor("x2_out", [TC, D], F32, kind="ExternalOutput").ap()

    NKT = T // 128

    with ExitStack() as ctx:
        sb = lambda n, s, d=F32: ctx.enter_context(nc.sbuf_tensor(n, s, d))
        ps = lambda n, s, d=F32: ctx.enter_context(nc.psum_tensor(n, s, d))
        sem = lambda n: ctx.enter_context(nc.semaphore(n))

        xreg = sb("xreg", [128, NT * D])
        x_sb = xreg
        tmp_f = xreg[:, 0:1024]
        ta = xreg[0:32, 1024:2048]
        tb = xreg[0:32, 2048:3072]
        thi = xreg[0:32, 3072:4096]
        cos_lo = xreg[0:32, 4096:5120]
        sin_lo = xreg[0:32, 5120:6144]
        cos_hi = xreg[0:32, 6144:7168]
        sin_hi = xreg[0:32, 7168:8192]

        raws = sb("raws", [128, NT]); m_cols = sb("m_cols", [128, NT])
        v_cols = sb("v_cols", [128, NT]); r_cols = sb("r_cols", [128, NT])
        s2_cols = sb("s2_cols", [128, NT]); eps_col = sb("eps_col", [128, 1])
        negM = sb("negM", [128, NH])
        kn_row = sb("kn_row", [1, NCORES * NH])
        nrow = sb("nrow", [1, 8 * NH])
        rs_cols = sb("rs_cols", [128, 8]); rq_cols = sb("rq_cols", [128, 8])

        ident = sb("ident", [128, 128], BF16)
        ones_col = sb("ones_col", [128, 1], BF16)
        ones_row = sb("ones_row", [1, 128], BF16)
        nrow_bf = sb("nrow_bf", [1, NH], BF16)

        regA = sb("regA", [128, NT * D], BF16)
        u_sb = regA; qT = regA; hT = regA
        regB = sb("regB", [128, NDM * TC], BF16)
        uT = regB
        ktb = [regB[:, 0:2048], regB[:, 2048:4096]]
        vga = [regB[:, 4096:5128], regB[:, 5128:6160]]
        pt = regB[:, 6160:10256]
        regC = sb("regC", [128, 6144], BF16)
        wcol = [regC[:, 0:2048], regC[:, 2048:4096]]
        qsq = [regC[:, 4096:5120], regC[:, 5120:6144]]
        hg = regC[:, 4096:6144]
        regD = sb("regD", [128, NT * D], BF16)
        ya = regD
        wbig = regD[:, 0:8 * D]
        regE = sb("regE", [128, NH * TC], BF16)
        yT = regE

        A = [ps(f"A{i}", [128, 512]) for i in range(4)]
        B = [ps(f"B{i}", [128, 258]) for i in range(4)]

        ld = sem("ld"); tbl = sem("tbl"); xl2 = sem("xl2")
        wl = [sem("wl0"), sem("wl1")]; wf = sem("wf")
        lns = sem("lns"); trs = sem("trs"); tre = sem("tre")
        prj = sem("prj"); evc = sem("evc"); ev2 = sem("ev2")
        sqd = sem("sqd"); qn1 = sem("qn1"); qn2 = sem("qn2")
        nmg1 = sem("nmg1"); nmg2 = sem("nmg2"); nmg3 = sem("nmg3")
        bc_t = sem("bc_t"); bc_v = sem("bc_v"); vgz = sem("vgz")
        kl = [sem("kl0"), sem("kl1")]; vl = [sem("vl0"), sem("vl1")]
        sprj = sem("sprj"); sexp = sem("sexp"); savc = sem("savc")
        yhs = sem("yhs"); ytr = sem("ytr"); yte = sem("yte")
        wol = [sem("wol0"), sem("wol1")]; woa = sem("woa"); wop = sem("wop")
        gl = [sem("gl0"), sem("gl1")]; gf = sem("gf")
        fpr = sem("fpr"); fsi = sem("fsi"); fev = sem("fev")
        dl = [sem("dl0"), sem("dl1")]; dpr = sem("dpr"); dad = sem("dad")
        out_s = sem("out_s")

        block = ctx.enter_context(nc.Block())
        n_tr = NDM * NT // 4

        cfg = {
            "wguT_d": wguT_d, "wdT_d": wdT_d, "uT": uT, "hT": hT,
            "wcol": wcol, "wbig": wbig, "hg": hg, "x_sb": x_sb, "A": A,
            "gl": gl, "gf": gf, "fpr": fpr, "fsi": fsi, "fev": fev,
            "dl": dl, "dpr": dpr, "dad": dad,
            "gf0": 0, "dad0": 0, "fev0": 0, "fpr0": 0, "fsi0": 0, "dpr0": 0,
            "gl0": [0, 0], "dl0": [0, 0],
        }

        @block.sync
        def _(sync):
            sync.dma_start(out=x_sb[:].rearrange("p (t d) -> p t d", t=NT),
                           in_=xs_d.rearrange("(t p) d -> p t d", p=128)
                           ).then_inc(ld, 16)
            sync.dma_start(out=ident[:], in_=id_d).then_inc(ld, 16)
            sync.dma_start(out=kn_row[:], in_=kn_d).then_inc(ld, 16)
            sync.wait_ge(lns, 3)
            sync.dma_start(out=cos_lo, in_=cos_d[0:32, :]).then_inc(tbl, 16)
            sync.dma_start(out=sin_lo, in_=sin_d[0:32, :]).then_inc(tbl, 16)
            sync.dma_start(out=cos_hi, in_=cos_d[32:64, :]).then_inc(tbl, 16)
            sync.dma_start(out=sin_hi, in_=sin_d[32:64, :]).then_inc(tbl, 16)
            sync.wait_ge(vgz, 1)
            for ph in range(2 * NH):
                h = ph // 2
                order = []
                for cb in range(4):
                    order.append(("k", cb))
                    order.append(("v", 2 * cb))
                    order.append(("v", 2 * cb + 1))
                for kind, idx in order:
                    if kind == "k":
                        gk = 4 * ph + idx
                        if gk >= 2:
                            sync.wait_ge(savc, max(0, 4 * (gk - 1)))
                        sync.dma_start(out=ktb[gk % 2],
                                       in_=ktf_d[h * 128:(h + 1) * 128,
                                                 idx * 2048:(idx + 1) * 2048]
                                       ).then_inc(kl[gk % 2], 16)
                    else:
                        gv = 8 * ph + idx
                        if gv >= 2:
                            sync.wait_ge(savc, max(0, 2 * (gv - 1)))
                        sync.dma_start(
                            out=vga[gv % 2].rearrange("p (k c) -> p k c", k=8)[:, :, 0:128],
                            in_=vh_d[h].rearrange("p (k c) -> p k c", k=NKT)
                                [:, idx * 8:(idx + 1) * 8, :],
                        ).then_inc(vl[gv % 2], 16)
            sync.wait_ge(yhs, 2 * NH)
            sync.dma_start(out=x_sb[:].rearrange("p (t d) -> p t d", t=NT),
                           in_=xs_d.rearrange("(t p) d -> p t d", p=128)
                           ).then_inc(xl2, 16)
            sync.wait_ge(out_s, 1)
            sync.dma_start(out=x2_d.rearrange("(t p) d -> p t d", p=128),
                           in_=x_sb[:].rearrange("p (t d) -> p t d", t=NT)
                           ).then_inc(ld, 16)
            sync.wait_ge(ld, 64)

        @block.gpsimd
        def _(g):
            for r in range(NDM):
                if r >= 2:
                    g.wait_ge(wf, r - 1)
                g.dma_start(
                    out=wcol[r % 2].rearrange("p (k c) -> p k c", k=NDM),
                    in_=wqT_d[:, r * 128:(r + 1) * 128]
                        .rearrange("(k p) c -> p k c", p=128),
                ).then_inc(wl[r % 2], 16)
            g.wait_ge(ytr, n_tr)
            for blk in range(2):
                if blk == 1:
                    g.wait_ge(wop, 8)
                g.dma_start(
                    out=wbig.rearrange("p (k c) -> p k c", k=8),
                    in_=woT_d[blk * 1024:(blk + 1) * 1024, :]
                        .rearrange("(k p) c -> p k c", p=128),
                ).then_inc(wol[blk], 16)
            g.wait_ge(woa, 16)
            ffn_gpsimd(g, cfg)

        @block.vector
        def _(v):
            v.memset(ones_col[:], 1.0)
            v.memset(ones_row[:], 1.0)
            v.memset(eps_col[:], EPS)
            v.wait_ge(ld, 48)
            v.drain()
            _ln_pre(v, x_sb, u_sb, NT, D, raws, m_cols, v_cols, s2_cols)
            v.drain().then_inc(lns, 1)

        @block.scalar
        def _(s):
            s.wait_ge(lns, 1)
            _ln_mid(s, NT, v_cols, r_cols, eps_col[:])
            s.drain().then_inc(lns, 1)

        @block.vector
        def _(v):
            v.wait_ge(lns, 2)
            _ln_post(v, x_sb, u_sb, NT, D, m_cols, r_cols)
            v.drain().then_inc(lns, 1)

        @block.tensor
        def _(t):
            t.wait_ge(ld, 48)
            t.wait_ge(lns, 3)
            for r in range(n_tr):
                dmt, tt4 = divmod(r, NT // 4)
                if r >= 2:
                    t.wait_ge(tre, r - 1)
                for j in range(4):
                    tt = tt4 * 4 + j
                    t.matmul(A[r % 2][:, j * 128:(j + 1) * 128],
                             u_sb[:, tt * D + dmt * 128:tt * D + (dmt + 1) * 128],
                             ident[:], start=True, stop=True)
                t.drain().then_inc(trs, 1)
            t.wait_ge(tre, n_tr)
            for r in range(NH + 2):
                if r < NH:
                    wb = wcol[r % 2]
                    t.wait_ge(wl[r % 2], 16 * (r // 2 + 1))
                    if r >= 2:
                        t.wait_ge(evc, r - 1)
                    for hhalf in range(2):
                        for k in range(NDM):
                            t.matmul(A[2 * (r % 2) + hhalf][:],
                                     wb[:, k * 128:(k + 1) * 128],
                                     uT[:, k * TC + hhalf * 512:
                                        k * TC + hhalf * 512 + 512],
                                     start=(k == 0), stop=(k == NDM - 1))
                    t.drain().then_inc(prj, 1)
                    t.drain().then_inc(wf, 1)
                if r >= 2:
                    t.wait_ge(sqd, r - 1)
                    if r >= 3:
                        t.wait_ge(qn2, r - 2)
                    for ch in range(4):
                        t.matmul(B[ch][0:1, 0:256], ones_col[:],
                                 qsq[r % 2][:, ch * 256:(ch + 1) * 256],
                                 start=True, stop=True).then_inc(qn1, 1)
            t.wait_ge(nmg3, 1)
            for grp in range(4):
                if grp >= 1:
                    t.wait_ge(bc_v, grp)
                for hh in range(4):
                    h = grp * 4 + hh
                    t.matmul(B[hh][:, 0:1], ones_row[:],
                             nrow_bf[0:1, h:h + 1], start=True, stop=True)
                t.drain().then_inc(bc_t, 1)
            t.wait_ge(bc_v, 4)
            for ph in range(2 * NH):
                h, qp = divmod(ph, 2)
                hb = 64 * ph
                if ph >= 1:
                    t.wait_ge(yhs, ph)

                def st_mm(kt):
                    gk = 4 * ph + kt // 16
                    t.wait_ge(kl[gk % 2], 16 * (gk // 2 + 1))
                    t.matmul(A[kt % 2][:],
                             ktb[gk % 2][:, (kt % 16) * 128:(kt % 16) * 128 + 128],
                             qT[:, h * TC + qp * 512:h * TC + qp * 512 + 512],
                             start=True, stop=True).then_inc(sprj, 1)
                st_mm(0)
                for kt in range(NKT):
                    if kt < NKT - 1:
                        if kt >= 1:
                            t.wait_ge(sexp, hb + kt)
                        st_mm(kt + 1)
                    t.wait_ge(sexp, hb + kt + 1)
                    gv = 8 * ph + kt // 8
                    t.wait_ge(vl[gv % 2], 16 * (gv // 2 + 1))
                    for qs in range(4):
                        t.matmul(B[qs][:, 0:129],
                                 pt[:, (kt % 4) * TC + qp * 512 + qs * 128:
                                    (kt % 4) * TC + qp * 512 + qs * 128 + 128],
                                 vga[gv % 2][:, (kt % 8) * 129:(kt % 8) * 129 + 129],
                                 start=(kt == 0), stop=(kt == NKT - 1))
                    if kt % 4 == 3:
                        t.drain().then_inc(savc, 1)
            t.wait_ge(yhs, 2 * NH)
            for r in range(n_tr):
                if r >= 2:
                    t.wait_ge(yte, r - 1)
                for j in range(4):
                    blk_i = r * 4 + j
                    h, qs = divmod(blk_i, 8)
                    t.matmul(A[r % 2][:, j * 128:(j + 1) * 128],
                             ya[:, h * TC + qs * 128:h * TC + qs * 128 + 128],
                             ident[:], start=True, stop=True)
                t.drain().then_inc(ytr, 1)
            t.wait_ge(yte, n_tr)
            rnd = 0
            for blk in range(2):
                t.wait_ge(wol[blk], 16)
                for ts in range(8):
                    if rnd >= 1:
                        t.wait_ge(woa, rnd)
                    for hd8 in range(8):
                        hd = blk * 8 + hd8
                        for n in range(4):
                            t.matmul(A[n][:],
                                     yT[:, hd * TC + ts * 128:hd * TC + ts * 128 + 128],
                                     wbig[:, hd8 * D + n * 512:hd8 * D + n * 512 + 512],
                                     start=(hd8 == 0), stop=(hd8 == 7))
                    t.drain().then_inc(wop, 1)
                    rnd += 1
            t.wait_ge(lns, 6)
            for r in range(n_tr):
                if r >= 2:
                    t.wait_ge(tre, n_tr + r - 1)
                dmt, tt4 = divmod(r, NT // 4)
                for j in range(4):
                    tt = tt4 * 4 + j
                    t.matmul(A[r % 2][:, j * 128:(j + 1) * 128],
                             u_sb[:, tt * D + dmt * 128:tt * D + (dmt + 1) * 128],
                             ident[:], start=True, stop=True)
                t.drain().then_inc(trs, 1)
            t.wait_ge(tre, 2 * n_tr)
            ffn_tensor(t, cfg)

        @block.vector
        def _(v):
            for r in range(n_tr):
                v.wait_ge(trs, r + 1)
                dmt, tt4 = divmod(r, NT // 4)
                v.tensor_copy(
                    uT[:, dmt * TC + tt4 * 512:dmt * TC + (tt4 + 1) * 512],
                    A[r % 2][:])
                v.drain().then_inc(tre, 1)
            v.wait_ge(tbl, 64)
            for r in range(NH + 2):
                if r < NH:
                    v.wait_ge(prj, r + 1)
                    if r >= 1:
                        v.wait_ge(sqd, r)
                    v.tensor_scalar(out=tmp_f[:, 0:512], in0=A[2 * (r % 2)][:],
                                    scalar1=SCL, scalar2=None, op0=ALU.mult)
                    v.tensor_scalar(out=tmp_f[:, 512:1024],
                                    in0=A[2 * (r % 2) + 1][:],
                                    scalar1=SCL, scalar2=None, op0=ALU.mult)
                    v.drain().then_inc(evc, 1)
                    dstq = qT[:, r * TC:(r + 1) * TC]
                    v.tensor_copy(thi, tmp_f[32:64, :])
                    v.tensor_copy(dstq[64:128, :], tmp_f[64:128, :])
                    v.drain()
                    v.tensor_mul(ta, tmp_f[0:32, :], cos_lo)
                    v.tensor_mul(tb, thi, sin_lo)
                    v.drain()
                    v.tensor_sub(dstq[0:32, :], ta, tb)
                    v.drain()
                    v.tensor_mul(ta, thi, cos_hi)
                    v.tensor_mul(tb, tmp_f[0:32, :], sin_hi)
                    v.drain()
                    v.tensor_add(dstq[32:64, :], ta, tb)
                    v.drain().then_inc(ev2, 1)
                if r >= 2:
                    v.wait_ge(qn1, 4 * (r - 1))
                    for ch in range(4):
                        v.reduce_max(nrow[0:1, 4 * (r - 2) + ch:
                                          4 * (r - 2) + ch + 1],
                                     B[ch][0:1, 0:256], axis=AX)
                    v.drain().then_inc(qn2, 1)
            v.reduce_max(nrow[0:1, 64:80].rearrange("a (h o) -> a h o", o=1),
                         nrow[0:1, 0:64].rearrange("a (h c) -> a h c", c=4),
                         axis=AX)
            v.reduce_max(nrow[0:1, 80:96].rearrange("a (h o) -> a h o", o=1),
                         kn_row[:].rearrange("a (c h) -> a h c", h=NH),
                         axis=AX)
            v.drain().then_inc(nmg1, 1)
            v.wait_ge(nmg2, 1)
            v.tensor_mul(nrow[0:1, 96:112], nrow[0:1, 112:128],
                         nrow[0:1, 64:80])
            v.drain()
            v.tensor_scalar(out=nrow_bf[0:1, 0:NH], in0=nrow[0:1, 96:112],
                            scalar1=-1.0005, scalar2=None, op0=ALU.mult)
            v.drain().then_inc(nmg3, 1)
            for grp in range(4):
                v.wait_ge(bc_t, grp + 1)
                for hh in range(4):
                    h = grp * 4 + hh
                    v.tensor_copy(negM[:, h:h + 1], B[hh][:, 0:1])
                v.drain().then_inc(bc_v, 1)
            v.memset(vga[0].rearrange("p (k c) -> p k c", k=8)[:, :, 128:129], 1.0)
            v.memset(vga[1].rearrange("p (k c) -> p k c", k=8)[:, :, 128:129], 1.0)
            v.drain().then_inc(vgz, 1)
            for ph in range(2 * NH):
                h, qp = divmod(ph, 2)
                v.wait_ge(savc, 16 * (ph + 1))
                for qs in range(4):
                    v.tensor_scalar_max(rs_cols[:, qs:qs + 1],
                                        B[qs][:, 128:129], 1e-30)
                v.drain()
                for qs in range(4):
                    v.reciprocal(rq_cols[:, qs:qs + 1], rs_cols[:, qs:qs + 1])
                v.drain()
                for qs in range(4):
                    gq = qp * 4 + qs
                    v.tensor_scalar(
                        out=ya[:, h * TC + gq * 128:h * TC + gq * 128 + 128],
                        in0=B[qs][:, 0:128],
                        scalar1=rq_cols[:, qs:qs + 1], scalar2=None,
                        op0=ALU.mult)
                v.drain().then_inc(yhs, 1)
            for r in range(n_tr):
                v.wait_ge(ytr, r + 1)
                for j in range(4):
                    blk_i = r * 4 + j
                    h, qs = divmod(blk_i, 8)
                    v.tensor_copy(yT[:, h * TC + qs * 128:h * TC + qs * 128 + 128],
                                  A[r % 2][:, j * 128:(j + 1) * 128])
                v.drain().then_inc(yte, 1)
            v.wait_ge(xl2, 16)
            for rnd in range(16):
                v.wait_ge(wop, rnd + 1)
                ts = rnd % 8
                for n in range(4):
                    v.tensor_add(x_sb[:, ts * D + n * 512:ts * D + n * 512 + 512],
                                 x_sb[:, ts * D + n * 512:ts * D + n * 512 + 512],
                                 A[n][:])
                v.drain().then_inc(woa, 1)
            _ln_pre(v, x_sb, u_sb, NT, D, raws, m_cols, v_cols, s2_cols)
            v.drain().then_inc(lns, 1)
            v.wait_ge(lns, 5)
            _ln_post(v, x_sb, u_sb, NT, D, m_cols, r_cols)
            v.drain().then_inc(lns, 1)
            for r in range(n_tr):
                v.wait_ge(trs, n_tr + r + 1)
                dmt, tt4 = divmod(r, NT // 4)
                v.tensor_copy(
                    uT[:, dmt * TC + tt4 * 512:dmt * TC + (tt4 + 1) * 512],
                    A[r % 2][:])
                v.drain().then_inc(tre, 1)
            ffn_vector(v, cfg)
            v.drain().then_inc(out_s, 1)

        @block.scalar
        def _(s):
            for r in range(NH):
                s.wait_ge(ev2, r + 1)
                if r >= 2:
                    s.wait_ge(qn1, 4 * (r - 1))
                s.activation(qsq[r % 2][:], tmp_f[:], AF.Square)
                s.drain().then_inc(sqd, 1)
            s.wait_ge(nmg1, 1)
            s.activation(nrow[0:1, 112:128], nrow[0:1, 80:96], AF.Sqrt)
            s.activation(nrow[0:1, 64:80], nrow[0:1, 64:80], AF.Sqrt)
            s.drain().then_inc(nmg2, 1)
            s.wait_ge(bc_v, 4)
            for ph in range(2 * NH):
                h, qp = divmod(ph, 2)
                hb = 64 * ph
                for kt in range(NKT):
                    s.wait_ge(sprj, hb + kt + 1)
                    if kt >= 4:
                        s.wait_ge(savc, 16 * ph + kt // 4)
                    elif ph >= 1:
                        s.wait_ge(savc, 16 * ph)
                    s.activation(pt[:, (kt % 4) * TC + qp * 512:
                                    (kt % 4) * TC + qp * 512 + 512],
                                 A[kt % 2][:], AF.Exp,
                                 bias=negM[:, h:h + 1], scale=1.0)
                    s.drain().then_inc(sexp, 1)
            s.wait_ge(lns, 4)
            _ln_mid(s, NT, v_cols, r_cols, eps_col[:])
            s.drain().then_inc(lns, 1)
            ffn_scalar(s, cfg)

    return nc



NCH_G = TC // CH


def build_C():
    nc = bass.Bass("TRN2", target_bir_lowering=False)
    x2_d = nc.dram_tensor("x2f", [T, D], F32, kind="ExternalInput").ap()
    wxT_d = nc.dram_tensor("wxT", [D, 512], F32, kind="ExternalInput").ap()
    wzT_d = nc.dram_tensor("wzT", [D, 512], F32, kind="ExternalInput").ap()
    wbcT_d = nc.dram_tensor("wbcT", [D, 256], F32, kind="ExternalInput").ap()
    wdtT_d = nc.dram_tensor("wdtT", [D, 8], F32, kind="ExternalInput").ap()
    cwx_d = nc.dram_tensor("convX", [512, DCONV], F32, kind="ExternalInput").ap()
    cwbc_d = nc.dram_tensor("convBC", [256, DCONV], F32, kind="ExternalInput").ap()
    cbx_d = nc.dram_tensor("cbX", [512, 1], F32, kind="ExternalInput").ap()
    cbbc_d = nc.dram_tensor("cbBC", [256, 1], F32, kind="ExternalInput").ap()
    dtb_d = nc.dram_tensor("dtb8", [8, 1], F32, kind="ExternalInput").ap()
    alog_d = nc.dram_tensor("alog8", [8, 1], F32, kind="ExternalInput").ap()
    dp_d = nc.dram_tensor("dp8", [8, 1], F32, kind="ExternalInput").ap()
    i256_d = nc.dram_tensor("i256", [256, 256], BF16, kind="ExternalInput").ap()
    mneg_d = nc.dram_tensor("maskneg", [256, 256], F32, kind="ExternalInput").ap()
    id_d = nc.dram_tensor("identin", [128, 128], BF16, kind="ExternalInput").ap()
    idf_d = nc.dram_tensor("identf32", [128, 128], F32, kind="ExternalInput").ap()
    yt_d = nc.dram_tensor("yt_out", [512, T], BF16, kind="ExternalOutput").ap()
    zt_d = nc.dram_tensor("zt_out", [512, T], BF16, kind="ExternalOutput").ap()
    dacs_hd = nc.dram_tensor("dacs_hbm", [NCH_G, 8, CH], F32, kind="Internal").ap()

    NSEC = T // TC
    NCH = TC // CH
    PB = [("x", i) for i in range(4)] + [("z", i) for i in range(4)] + \
         [("bc", i) for i in range(2)]
    TC3 = TC + 3
    NHR = 22

    with ExitStack() as ctx:
        sb = lambda n, s, d=F32: ctx.enter_context(nc.sbuf_tensor(n, s, d))
        ps = lambda n, s, d=F32: ctx.enter_context(nc.psum_tensor(n, s, d))
        sem = lambda n: ctx.enter_context(nc.semaphore(n))

        xtile = [sb(f"xtile{i}", [128, D]) for i in range(2)]
        u_sb = sb("u_sb", [128, NT * D], BF16)
        uT = sb("uT", [128, NDM * TC], BF16)
        wblk = [sb(f"wblk{i}", [128, NDM * 128], BF16) for i in range(2)]
        wdtb = sb("wdtb", [128, NDM * 8], BF16)
        ident = sb("ident", [128, 128], BF16)
        identf = sb("identf", [128, 128])
        ones_row = sb("ones_row", [1, 128], BF16)
        ones_rowf = sb("ones_rowf", [1, 128])
        zsec = sb("zsec", [128, 4 * TC], BF16)
        xcT = sb("xcT", [128, 4 * TC3], BF16)
        bcT = sb("bcT", [128, 2 * TC3], BF16)
        xsT = sb("xsT", [128, 4 * TC], BF16)
        bsT = sb("bsT", [128, 2 * TC], BF16)
        x_nt = sb("x_nt", [128, NT * 512], BF16)
        b_nt = sb("b_nt", [128, NT * 128], BF16)
        yTsec = sb("yTsec", [128, 4 * TC], BF16)
        cacc = sb("cacc", [128, TC])
        cwx = sb("cwx", [128, 4 * DCONV])
        cwbc = sb("cwbc", [128, 2 * DCONV])
        cbx = sb("cbx", [128, 4])
        cbbc = sb("cbbc", [128, 2])
        dtb8 = sb("dtb8_s", [8, 1]); alog8 = sb("alog8_s", [8, 1])
        dp8 = sb("dp8_s", [8, 1]); nega = sb("nega", [8, 1])
        ones8 = sb("ones8", [8, 1]); zer8 = sb("zer8", [8, CH])
        dtT = sb("dtT", [8, TC]); dAT = sb("dAT", [8, TC])
        dAcsT = sb("dAcsT", [8, TC]); wtT = sb("wtT", [8, TC])
        sp_r = sb("sp_r", [8, TC]); sp_m = sb("sp_m", [8, TC])
        dt_nt = sb("dt_nt", [128, NT * 8]); wt_nt = sb("wt_nt", [128, NT * 8])
        da_nt = sb("da_nt", [128, NT * 8])
        i2 = sb("i2", [128, 2 * CH], BF16)
        dpi = sb("dpi", [128, 8 * 2 * CH], BF16)
        dprep = sb("dprep", [128, 8])
        masks = sb("masks", [128, 2 * CH])
        rc = sb("rc", [1, 8 * CH])
        seg_f = [sb(f"seg_f{i}", [128, 2 * CH]) for i in range(2)]
        e_sb = [sb(f"e_sb{i}", [128, 2 * CH], BF16) for i in range(2)]
        cb_sb = sb("cb_sb", [128, 2 * CH], BF16)
        sco = [sb(f"sco{i}", [128, 2 * CH], BF16) for i in range(2)]
        cef = [sb(f"cef{i}", [128, CH], BF16) for i in range(2)]
        expf_bf = [sb(f"expf{i}", [128, CH], BF16) for i in range(2)]
        bw_sb = [sb(f"bw_sb{i}", [128, 2 * 128], BF16) for i in range(2)]
        state = sb("state", [128, 8 * 64])
        h_bf = sb("h_bf", [128, 8 * 64], BF16)
        drow_bf = sb("drow_bf", [1, 8], BF16)
        drep = sb("drep", [128, 8])
        raws = sb("raws", [128, NT]); m_cols = sb("m_cols", [128, NT])
        v_cols = sb("v_cols", [128, NT]); r_cols = sb("r_cols", [128, NT])
        s2_cols = sb("s2_cols", [128, NT]); eps_col = sb("eps_col", [128, 1])

        P1 = [ps(f"P1{i}", [128, 512]) for i in range(2)]
        P2 = [ps(f"P2{i}", [128, CH]) for i in range(3)]
        P3 = [ps(f"P3{i}", [128, CH]) for i in range(2)]

        ld = sem("ld"); xl = [sem("xl0"), sem("xl1")]
        lnt = sem("lnt"); lns = sem("lns"); lnp = sem("lnp")
        trs = sem("trs"); tre = sem("tre")
        prj = sem("prj"); evc = sem("evc")
        cvv = sem("cvv"); cvs = sem("cvs")
        dts = sem("dts"); dtv = sem("dtv"); dms = sem("dms")
        bnc = sem("bnc"); lrl = sem("lrl")
        bcm = sem("bcm"); sgv = sem("sgv"); sge = sem("sge")
        see = sem("see"); scs = sem("scs")
        cbm = sem("cbm"); cbe = sem("cbe")
        ymm = sem("ymm"); yev = sem("yev")
        stm = sem("stm"); stu = sem("stu"); hbc = sem("hbc")
        drm = sem("drm"); drv = sem("drv")
        dpm = sem("dpm"); dpv = sem("dpv")
        ini = sem("ini"); pre = sem("pre")
        zo = sem("zo"); yo = sem("yo")

        block = ctx.enter_context(nc.Block())
        n_tr = NDM * NT // 4
        NTR2 = 32 + 8 + 24

        @block.sync
        def _(sync):
            sync.dma_start(out=ident[:], in_=id_d).then_inc(ld, 16)
            sync.dma_start(out=identf[:], in_=idf_d).then_inc(ld, 16)
            sync.dma_start(out=cwx[:].rearrange("p (k c) -> p k c", k=4),
                           in_=cwx_d.rearrange("(k p) c -> p k c", p=128)
                           ).then_inc(ld, 16)
            sync.dma_start(out=cwbc[:].rearrange("p (k c) -> p k c", k=2),
                           in_=cwbc_d.rearrange("(k p) c -> p k c", p=128)
                           ).then_inc(ld, 16)
            sync.dma_start(out=cbx[:].rearrange("p (k c) -> p k c", k=4),
                           in_=cbx_d.rearrange("(k p) c -> p k c", p=128),
                           allow_slow_non_contiguous=True).then_inc(ld, 16)
            sync.dma_start(out=cbbc[:].rearrange("p (k c) -> p k c", k=2),
                           in_=cbbc_d.rearrange("(k p) c -> p k c", p=128),
                           allow_slow_non_contiguous=True).then_inc(ld, 16)
            sync.dma_start(out=dtb8[:], in_=dtb_d).then_inc(ld, 16)
            sync.dma_start(out=alog8[:], in_=alog_d).then_inc(ld, 16)
            sync.dma_start(out=dp8[:], in_=dp_d).then_inc(ld, 16)
            sync.dma_start(out=masks[:].rearrange("p (k c) -> p k c", k=2),
                           in_=mneg_d.rearrange("(k p) c -> p k c", p=128)
                           ).then_inc(ld, 16)
            sync.dma_start(out=i2[:].rearrange("p (k c) -> p k c", k=2),
                           in_=i256_d.rearrange("(k p) c -> p k c", p=128)
                           ).then_inc(ld, 16)
            for s in range(NSEC):
                for t in range(NT):
                    g = s * NT + t
                    if g >= 2:
                        sync.wait_ge(lnt, g - 1)
                    sync.dma_start(out=xtile[g % 2][:],
                                   in_=x2_d[g * 128:(g + 1) * 128, :]
                                   ).then_inc(xl[g % 2], 16)
                sync.wait_ge(dms, s + 1)
                sync.dma_start(out=dacs_hd.rearrange("c h t -> h c t"),
                               in_=dAcsT[:].rearrange("p (c t) -> p c t", c=NCH)
                               ).then_inc(bnc, 16)
                sync.wait_ge(bnc, 16 * (s + 1))
                for j in range(NCH):
                    gc = s * NCH + j
                    if gc >= 1:
                        sync.wait_ge(bcm, 8 * gc)
                    sync.dma_start(
                        out=rc[0:1, :].rearrange("a (h t) -> a h t", h=8),
                        in_=dacs_hd[j:j + 1, :, :],
                    ).then_inc(lrl, 16)
                sync.wait_ge(evc, NHR * s + 16)
                sync.dma_start(
                    out=zt_d[:, s * TC:(s + 1) * TC]
                        .rearrange("(k p) t -> p k t", p=128),
                    in_=zsec[:].rearrange("p (k t) -> p k t", k=4),
                ).then_inc(zo, 16)
                sync.wait_ge(yev, 32 * (s + 1))
                sync.dma_start(
                    out=yt_d[:, s * TC:(s + 1) * TC]
                        .rearrange("(k p) t -> p k t", p=128),
                    in_=yTsec[:].rearrange("p (k t) -> p k t", k=4),
                ).then_inc(yo, 16)
            sync.wait_ge(zo, 16 * NSEC)
            sync.wait_ge(yo, 16 * NSEC)

        wl = [sem("wl0"), sem("wl1")]; wf = sem("wf")

        wdl = sem("wdl")

        @block.gpsimd
        def _(g):
            g.dma_start(out=wdtb[:].rearrange("p (k c) -> p k c", k=NDM),
                        in_=wdtT_d.rearrange("(k p) c -> p k c", p=128)
                        ).then_inc(wdl, 16)
            for se in range(NSEC):
                for bi, (kind, i) in enumerate(PB):
                    gw = 10 * se + bi
                    if gw >= 2:
                        g.wait_ge(wf, gw - 1)
                    srcw = {"x": wxT_d, "z": wzT_d, "bc": wbcT_d}[kind]
                    g.dma_start(
                        out=wblk[gw % 2][:].rearrange("p (k c) -> p k c", k=NDM),
                        in_=srcw[:, i * 128:(i + 1) * 128]
                            .rearrange("(k p) c -> p k c", p=128),
                    ).then_inc(wl[gw % 2], 16)

        LD_ALL = 16 * 11

        @block.scalar
        def _(s):
            s.wait_ge(ld, LD_ALL)
            s.wait_ge(ini, 1)
            s.activation(nega[:], alog8[:], AF.Exp)
            s.drain().then_inc(dts, 1)
            for se in range(NSEC):
                for t in range(NT):
                    g = se * NT + t
                    s.wait_ge(lns, g + 1)
                    s.activation(r_cols[:, t:t + 1], v_cols[:, t:t + 1],
                                 AF.Ln, bias=eps_col[:], scale=1.0)
                    s.drain()
                    s.activation(r_cols[:, t:t + 1], r_cols[:, t:t + 1],
                                 AF.Exp, scale=-0.5)
                    s.drain().then_inc(lnp, 1)
                for ct in range(6):
                    s.wait_ge(cvv, 6 * se + ct + 1)
                    if ct < 4:
                        s.activation(xsT[:, ct * TC:(ct + 1) * TC], cacc[:],
                                     AF.Silu, bias=cbx[:, ct:ct + 1], scale=1.0)
                    else:
                        s.activation(bsT[:, (ct - 4) * TC:(ct - 3) * TC],
                                     cacc[:], AF.Silu,
                                     bias=cbbc[:, ct - 4:ct - 3], scale=1.0)
                    s.drain().then_inc(cvs, 1)
                s.wait_ge(evc, NHR * se + NHR)
                s.activation(sp_r[:], dtT[:], AF.Relu)
                s.drain().then_inc(dts, 1)
                s.wait_ge(dtv, 2 * se + 2)
                s.activation(sp_m[:], sp_m[:], AF.Exp)
                s.drain()
                s.activation(sp_m[:], sp_m[:], AF.Ln, bias=ones8[:], scale=1.0)
                s.drain().then_inc(dts, 1)
                for j in range(NCH):
                    s.wait_ge(dtv, 2 * se + 3)
                    s.activation(wtT[:, j * CH:(j + 1) * CH],
                                 wtT[:, j * CH:(j + 1) * CH], AF.Exp, scale=-1.0)
                s.drain().then_inc(dts, 1)
                for j in range(NCH):
                    gc = se * NCH + j
                    s.wait_ge(drm, 2 * gc + 1)
                    s.activation(drow_bf[0:1, 0:8], P1[1][0:1, 0:8], AF.Exp)
                    s.drain().then_inc(drv, 1)
                    for h in range(8):
                        idx = 8 * gc + h
                        s.wait_ge(bcm, idx + 1)
                        if idx >= 2:
                            s.wait_ge(scs, idx - 1)
                        s.activation(expf_bf[h % 2][:], P2[h % 2][:], AF.Exp)
                        s.drain().then_inc(sge, 1)
                        s.wait_ge(sgv, idx + 1)
                        s.activation(e_sb[h % 2][:], seg_f[h % 2][:], AF.Exp)
                        s.drain().then_inc(see, 1)

        @block.tensor
        def _(t):
            t.wait_ge(ld, LD_ALL)
            t.wait_ge(wdl, 16)
            t.wait_ge(ini, 1)
            t.transpose(P1[1][0:1, 0:8], dp8[:], identf[0:8, 0:8])
            t.drain().then_inc(dpm, 1)
            t.wait_ge(dpv, 1)
            t.matmul(P1[0][:, 0:8], ones_row[:], drow_bf[0:1, 0:8],
                     start=True, stop=True)
            t.drain().then_inc(dpm, 1)
            t.wait_ge(dpv, 2)
            TRS_SEC = n_tr + (NTR2 + 3) // 4
            Rg = 0
            for se in range(NSEC):
                t.wait_ge(lnt, NT * (se + 1))
                if se >= 1:
                    t.wait_ge(stu, 8 * NCH * se)
                for r in range(n_tr):
                    dmt, tt4 = divmod(r, NT // 4)
                    Rg += 1
                    if Rg >= 3:
                        t.wait_ge(tre, Rg - 2)
                    for jj in range(4):
                        tt = tt4 * 4 + jj
                        t.matmul(P1[Rg % 2][:, jj * 128:(jj + 1) * 128],
                                 u_sb[:, tt * D + dmt * 128:
                                      tt * D + (dmt + 1) * 128],
                                 ident[:], start=True, stop=True)
                    t.drain().then_inc(trs, 1)
                t.wait_ge(tre, TRS_SEC * se + n_tr)
                for hr in range(NHR):
                    bi, half = hr // 2, hr % 2
                    if hr >= 2:
                        t.wait_ge(evc, NHR * se + hr - 1)
                    if bi < 10:
                        gw = 10 * se + bi
                        t.wait_ge(wl[gw % 2], 16 * (gw // 2 + 1))
                        wb = wblk[gw % 2]
                        for k in range(NDM):
                            t.matmul(P1[hr % 2][:],
                                     wb[:, k * 128:(k + 1) * 128],
                                     uT[:, k * TC + half * 512:
                                        k * TC + half * 512 + 512],
                                     start=(k == 0), stop=(k == NDM - 1))
                        if half == 1:
                            t.drain().then_inc(wf, 1)
                    else:
                        for k in range(NDM):
                            t.matmul(P1[hr % 2][0:8, :],
                                     wdtb[:, k * 8:(k + 1) * 8],
                                     uT[:, k * TC + half * 512:
                                        k * TC + half * 512 + 512],
                                     start=(k == 0), stop=(k == NDM - 1))
                    t.drain().then_inc(prj, 1)
                t.wait_ge(cvs, 6 * (se + 1))
                t.wait_ge(dts, 3 * (se + 1) + 1)
                for grp in range((NTR2 + 3) // 4):
                    Rg += 1
                    if Rg >= 3:
                        t.wait_ge(tre, Rg - 2)
                    for jj in range(4):
                        bi2 = grp * 4 + jj
                        if bi2 >= NTR2:
                            continue
                        pslice = P1[Rg % 2][:, jj * 128:(jj + 1) * 128]
                        if bi2 < 32:
                            ct, tt = divmod(bi2, 8)
                            t.matmul(pslice,
                                     xsT[:, ct * TC + tt * 128:
                                         ct * TC + (tt + 1) * 128],
                                     ident[:], start=True, stop=True)
                        elif bi2 < 40:
                            tt = bi2 - 32
                            t.matmul(pslice,
                                     bsT[:, tt * 128:(tt + 1) * 128],
                                     ident[:], start=True, stop=True)
                        else:
                            kk, tt = divmod(bi2 - 40, NT)
                            srcT = [dtT, wtT, dAcsT][kk]
                            t.transpose(pslice[:, 0:8],
                                        srcT[:, tt * 128:(tt + 1) * 128],
                                        identf[0:8, 0:8])
                    t.drain().then_inc(trs, 1)
                for j in range(NCH):
                    gc = se * NCH + j
                    t.wait_ge(stu, 8 * gc)
                    t.wait_ge(tre, TRS_SEC * (se + 1))
                    t.wait_ge(dms, se + 1)
                    t.transpose(P1[1][0:1, 0:8],
                                dAcsT[:, j * CH + CH - 1:j * CH + CH],
                                identf[0:8, 0:8])
                    t.drain().then_inc(drm, 1)
                    t.wait_ge(drv, 2 * gc + 1)
                    t.matmul(P1[0][:, 0:8], ones_row[:], drow_bf[0:1, 0:8],
                             start=True, stop=True)
                    t.drain().then_inc(drm, 1)
                    for jt in range(2):
                        t.wait_ge(cbe, 2 * gc + jt)
                        t.matmul(P2[2][:],
                                 bsT[:, j * CH + jt * 128:j * CH + jt * 128 + 128],
                                 bsT[:, TC + j * CH:TC + (j + 1) * CH],
                                 start=True, stop=True).then_inc(cbm, 1)
                    t.wait_ge(lrl, 16 * (gc + 1))

                    def bcast(h):
                        idx = 8 * gc + h
                        if idx >= 2:
                            t.wait_ge(sge, idx - 1)
                            t.wait_ge(sgv, idx - 1)
                        t.matmul(P2[h % 2][:], ones_rowf[:],
                                 rc[0:1, h * CH:(h + 1) * CH],
                                 start=True, stop=True).then_inc(bcm, 1)
                    bcast(0)
                    bcast(1)
                    for h in range(8):
                        idx = 8 * gc + h
                        if h + 2 < 8:
                            bcast(h + 2)
                        t.wait_ge(scs, idx + 1)
                        t.wait_ge(hbc, gc + 1)
                        t.wait_ge(drv, 2 * (gc + 1))
                        if idx >= 2:
                            t.wait_ge(yev, idx - 1)
                        tt0, tt1 = 2 * j, 2 * j + 1
                        t.matmul(P3[h % 2][0:64, 0:CH],
                                 x_nt[:, tt0 * 512 + 64 * h:tt0 * 512 + 64 * h + 64],
                                 sco[h % 2][:, 0:CH],
                                 start=True, stop=False)
                        t.matmul(P3[h % 2][0:64, 128:CH],
                                 x_nt[:, tt1 * 512 + 64 * h:tt1 * 512 + 64 * h + 64],
                                 sco[h % 2][:, CH + 128:2 * CH],
                                 start=False, stop=False)
                        for jt in range(2):
                            tt = 2 * j + jt
                            t.matmul(P3[h % 2][0:64, 0:CH],
                                     x_nt[:, tt * 512 + 64 * h:
                                          tt * 512 + 64 * h + 64],
                                     dpi[:, (2 * h + jt) * CH:
                                         (2 * h + jt + 1) * CH],
                                     start=False, stop=False)
                        t.matmul(P3[h % 2][0:64, 0:CH],
                                 h_bf[:, 64 * h:64 * h + 64],
                                 cef[h % 2][:], start=False, stop=True)
                        t.drain().then_inc(ymm, 1)
                        if idx >= 2:
                            t.wait_ge(stu, idx - 1)
                        for jt in range(2):
                            tt = 2 * j + jt
                            t.matmul(P1[h % 2][:, 0:64],
                                     bw_sb[h % 2][:, jt * 128:(jt + 1) * 128],
                                     x_nt[:, tt * 512 + 64 * h:
                                          tt * 512 + 64 * h + 64],
                                     start=(jt == 0), stop=(jt == 1))
                        t.drain().then_inc(stm, 1)

        @block.vector
        def _(v):
            v.memset(ones_row[:], 1.0)
            v.memset(ones_rowf[:], 1.0)
            v.memset(eps_col[:], EPS)
            v.memset(ones8[:], 1.0)
            v.memset(zer8[:], 0.0)
            v.memset(state[:], 0.0)
            v.memset(h_bf[:], 0.0)
            v.memset(rc[0:1, :], -1.0)
            v.memset(xcT[:].rearrange("p (k t) -> p k t", t=TC3)[:, :, 0:3], 0.0)
            v.memset(bcT[:].rearrange("p (k t) -> p k t", t=TC3)[:, :, 0:3], 0.0)
            v.drain().then_inc(ini, 1)
            v.drain().then_inc(hbc, 1)
            v.wait_ge(ld, LD_ALL)
            v.wait_ge(dts, 1)
            v.tensor_scalar_mul(nega[:], nega[:], -1.0)
            v.drain().then_inc(dtv, 1)
            v.wait_ge(dpm, 1)
            v.tensor_copy(drow_bf[0:1, 0:8], P1[1][0:1, 0:8])
            v.drain().then_inc(dpv, 1)
            v.wait_ge(dpm, 2)
            v.tensor_copy(dprep[:], P1[0][:, 0:8])
            v.drain()
            for h in range(8):
                for kk in range(2):
                    v.tensor_scalar(
                        out=dpi[:, (2 * h + kk) * CH:(2 * h + kk + 1) * CH],
                        in0=i2[:, kk * CH:(kk + 1) * CH],
                        scalar1=dprep[:, h:h + 1], scalar2=None, op0=ALU.mult)
            v.drain().then_inc(dpv, 1)
            v.drain().then_inc(pre, 1)
            for se in range(NSEC):
                for t_ in range(NT):
                    g = se * NT + t_
                    v.wait_ge(xl[g % 2], 16 * (g // 2 + 1))
                    xs = xtile[g % 2][:]
                    v.reduce_sum(raws[:, t_:t_ + 1], xs, axis=AX)
                    v.scalar_tensor_tensor(out=u_sb[:, t_ * D:(t_ + 1) * D],
                                           in0=xs, scalar=1.0, in1=xs,
                                           op0=ALU.mult, op1=ALU.mult,
                                           accum_out=v_cols[:, t_:t_ + 1])
                    v.drain()
                    v.tensor_scalar_mul(m_cols[:, t_:t_ + 1],
                                        raws[:, t_:t_ + 1], 1.0 / D)
                    v.scalar_tensor_tensor(out=s2_cols[:, t_:t_ + 1],
                                           in0=raws[:, t_:t_ + 1],
                                           scalar=1.0 / (D * D),
                                           in1=raws[:, t_:t_ + 1],
                                           op0=ALU.mult, op1=ALU.mult)
                    v.tensor_scalar_mul(v_cols[:, t_:t_ + 1],
                                        v_cols[:, t_:t_ + 1], 1.0 / D)
                    v.drain()
                    v.tensor_sub(v_cols[:, t_:t_ + 1], v_cols[:, t_:t_ + 1],
                                 s2_cols[:, t_:t_ + 1])
                    v.drain().then_inc(lns, 1)
                    v.wait_ge(lnp, g + 1)
                    v.tensor_scalar(out=u_sb[:, t_ * D:(t_ + 1) * D], in0=xs,
                                    scalar1=m_cols[:, t_:t_ + 1],
                                    scalar2=r_cols[:, t_:t_ + 1],
                                    op0=ALU.subtract, op1=ALU.mult)
                    v.drain().then_inc(lnt, 1)
                TRS_SEC = n_tr + (NTR2 + 3) // 4
                for r in range(n_tr):
                    Rv = TRS_SEC * se + r + 1
                    v.wait_ge(trs, Rv)
                    dmt, tt4 = divmod(r, NT // 4)
                    v.tensor_copy(
                        uT[:, dmt * TC + tt4 * 512:dmt * TC + (tt4 + 1) * 512],
                        P1[Rv % 2][:])
                    v.drain().then_inc(tre, 1)
                for hr in range(NHR):
                    bi, half = hr // 2, hr % 2
                    v.wait_ge(prj, NHR * se + hr + 1)
                    pp = P1[hr % 2][:]
                    if bi == 10:
                        v.tensor_scalar_add(
                            dtT[:, half * 512:(half + 1) * 512],
                            P1[hr % 2][0:8, :], dtb8[:])
                        v.drain().then_inc(evc, 1)
                        continue
                    if bi < 4:
                        if se >= 1 and hr == 0:
                            pass
                        v.tensor_copy(
                            xcT[:, bi * TC3 + 3 + half * 512:
                                bi * TC3 + 3 + half * 512 + 512], pp)
                    elif bi < 8:
                        if hr == 8 and se >= 1:
                            v.wait_ge(zo, 16 * se)
                        v.tensor_copy(
                            zsec[:, (bi - 4) * TC + half * 512:
                                 (bi - 4) * TC + half * 512 + 512], pp)
                    else:
                        v.tensor_copy(
                            bcT[:, (bi - 8) * TC3 + 3 + half * 512:
                                (bi - 8) * TC3 + 3 + half * 512 + 512], pp)
                    v.drain().then_inc(evc, 1)
                for ct in range(6):
                    src = xcT[:, ct * TC3:(ct + 1) * TC3] if ct < 4 else \
                        bcT[:, (ct - 4) * TC3:(ct - 3) * TC3]
                    cw = cwx[:, ct * DCONV:(ct + 1) * DCONV] if ct < 4 else \
                        cwbc[:, (ct - 4) * DCONV:(ct - 3) * DCONV]
                    if ct == 0:
                        v.wait_ge(evc, NHR * se + 8)
                    if ct == 4:
                        v.wait_ge(evc, NHR * se + 20)
                    v.wait_ge(cvs, 6 * se + ct)
                    v.tensor_scalar(out=cacc[:], in0=src[:, 0:TC],
                                    scalar1=cw[:, 0:1], scalar2=None,
                                    op0=ALU.mult)
                    v.drain()
                    for kk in range(1, DCONV):
                        v.scalar_tensor_tensor(out=cacc[:],
                                               in0=src[:, kk:kk + TC],
                                               scalar=cw[:, kk:kk + 1],
                                               in1=cacc[:], op0=ALU.mult,
                                               op1=ALU.add)
                        v.drain()
                    v.drain().then_inc(cvv, 1)
                v.tensor_copy(
                    xcT[:].rearrange("p (k t) -> p k t", t=TC3)[:, :, 0:3],
                    xcT[:].rearrange("p (k t) -> p k t", t=TC3)[:, :, TC:TC + 3])
                v.tensor_copy(
                    bcT[:].rearrange("p (k t) -> p k t", t=TC3)[:, :, 0:3],
                    bcT[:].rearrange("p (k t) -> p k t", t=TC3)[:, :, TC:TC + 3])
                v.drain()
                v.wait_ge(dts, 3 * se + 2)
                v.scalar_tensor_tensor(out=sp_m[:], in0=sp_r[:], scalar=-2.0,
                                       in1=dtT[:], op0=ALU.mult, op1=ALU.add)
                v.drain().then_inc(dtv, 1)
                v.wait_ge(dts, 3 * se + 3)
                v.tensor_add(dtT[:], sp_r[:], sp_m[:])
                v.drain()
                v.tensor_scalar(out=dAT[:], in0=dtT[:], scalar1=nega[:],
                                scalar2=None, op0=ALU.mult)
                v.drain()
                if se >= 1:
                    v.wait_ge(bnc, 16 * se)
                for j in range(NCH):
                    v.tensor_tensor_scan(
                        out=dAcsT[:, j * CH:(j + 1) * CH],
                        data0=dAT[:, j * CH:(j + 1) * CH],
                        data1=zer8[:], initial=0.0,
                        op0=ALU.add, op1=ALU.add)
                v.drain()
                for j in range(NCH):
                    v.tensor_scalar(
                        out=wtT[:, j * CH:(j + 1) * CH],
                        in0=dAcsT[:, j * CH:(j + 1) * CH],
                        scalar1=dAcsT[:, j * CH + CH - 1:j * CH + CH],
                        scalar2=None, op0=ALU.subtract)
                v.drain().then_inc(dtv, 1)
                v.wait_ge(dts, 3 * se + 4)
                for j in range(NCH):
                    v.tensor_mul(wtT[:, j * CH:(j + 1) * CH],
                                 wtT[:, j * CH:(j + 1) * CH],
                                 dtT[:, j * CH:(j + 1) * CH])
                v.drain().then_inc(dms, 1)
                for grp in range((NTR2 + 3) // 4):
                    Rv = TRS_SEC * se + n_tr + grp + 1
                    v.wait_ge(trs, Rv)
                    for jj in range(4):
                        bi2 = grp * 4 + jj
                        if bi2 >= NTR2:
                            continue
                        pslice = P1[Rv % 2][:, jj * 128:(jj + 1) * 128]
                        if bi2 < 32:
                            ct, tt = divmod(bi2, 8)
                            v.tensor_copy(
                                x_nt[:, tt * 512 + ct * 128:
                                     tt * 512 + (ct + 1) * 128], pslice)
                        elif bi2 < 40:
                            tt = bi2 - 32
                            v.tensor_copy(
                                b_nt[:, tt * 128:(tt + 1) * 128], pslice)
                        else:
                            kk, tt = divmod(bi2 - 40, NT)
                            dst = [dt_nt, wt_nt, da_nt][kk]
                            v.tensor_copy(dst[:, tt * 8:(tt + 1) * 8],
                                          pslice[:, 0:8])
                    v.drain().then_inc(tre, 1)
                for j in range(NCH):
                    gc = se * NCH + j
                    v.wait_ge(drm, 2 * gc + 2)
                    v.tensor_copy(drep[:], P1[0][:, 0:8])
                    v.drain().then_inc(drv, 1)
                    for jt in range(2):
                        v.wait_ge(cbm, 2 * gc + jt + 1)
                        if gc >= 1 and jt == 0:
                            v.wait_ge(scs, 8 * gc)
                        v.tensor_copy(cb_sb[:, jt * CH:(jt + 1) * CH], P2[2][:])
                        v.drain().then_inc(cbe, 1)
                    for h in range(8):
                        idx = 8 * gc + h
                        tt0, tt1 = 2 * j, 2 * j + 1
                        v.wait_ge(bcm, idx + 1)
                        v.wait_ge(sge, idx + 1)
                        if idx >= 2:
                            v.wait_ge(see, idx - 1)
                        v.scalar_tensor_tensor(
                            out=seg_f[h % 2][:, 0:CH], in0=P2[h % 2][:],
                            scalar=da_nt[:, tt0 * 8 + h:tt0 * 8 + h + 1],
                            in1=masks[:, 0:CH], op0=ALU.subtract, op1=ALU.add)
                        v.scalar_tensor_tensor(
                            out=seg_f[h % 2][:, CH:2 * CH], in0=P2[h % 2][:],
                            scalar=da_nt[:, tt1 * 8 + h:tt1 * 8 + h + 1],
                            in1=masks[:, CH:2 * CH], op0=ALU.subtract,
                            op1=ALU.add)
                        v.drain().then_inc(sgv, 1)
                        v.wait_ge(see, idx + 1)
                        if idx >= 2:
                            v.wait_ge(ymm, idx - 1)
                        v.scalar_tensor_tensor(
                            out=sco[h % 2][:, 0:CH], in0=cb_sb[:, 0:CH],
                            scalar=dt_nt[:, tt0 * 8 + h:tt0 * 8 + h + 1],
                            in1=e_sb[h % 2][:, 0:CH],
                            op0=ALU.mult, op1=ALU.mult)
                        v.scalar_tensor_tensor(
                            out=sco[h % 2][:, CH:2 * CH], in0=cb_sb[:, CH:2 * CH],
                            scalar=dt_nt[:, tt1 * 8 + h:tt1 * 8 + h + 1],
                            in1=e_sb[h % 2][:, CH:2 * CH],
                            op0=ALU.mult, op1=ALU.mult)
                        v.tensor_mul(cef[h % 2][:], expf_bf[h % 2][:],
                                     bsT[:, TC + j * CH:TC + (j + 1) * CH])
                        v.tensor_scalar(
                            out=bw_sb[h % 2][:, 0:128],
                            in0=b_nt[:, tt0 * 128:(tt0 + 1) * 128],
                            scalar1=wt_nt[:, tt0 * 8 + h:tt0 * 8 + h + 1],
                            scalar2=None, op0=ALU.mult)
                        v.tensor_scalar(
                            out=bw_sb[h % 2][:, 128:256],
                            in0=b_nt[:, tt1 * 128:(tt1 + 1) * 128],
                            scalar1=wt_nt[:, tt1 * 8 + h:tt1 * 8 + h + 1],
                            scalar2=None, op0=ALU.mult)
                        v.drain().then_inc(scs, 1)
                        v.wait_ge(ymm, idx + 1)
                        if se >= 1 and j == 0 and h == 0:
                            v.wait_ge(yo, 16 * se)
                        v.tensor_copy(
                            yTsec[64 * (h % 2):64 * (h % 2) + 64,
                                  (h // 2) * TC + j * CH:
                                  (h // 2) * TC + (j + 1) * CH],
                            P3[h % 2][0:64, 0:CH])
                        v.drain().then_inc(yev, 1)
                        v.wait_ge(stm, idx + 1)
                        v.scalar_tensor_tensor(
                            out=state[:, 64 * h:64 * h + 64],
                            in0=state[:, 64 * h:64 * h + 64],
                            scalar=drep[:, h:h + 1],
                            in1=P1[h % 2][:, 0:64], op0=ALU.mult, op1=ALU.add)
                        v.drain().then_inc(stu, 1)
                    v.wait_ge(stu, 8 * (gc + 1))
                    v.tensor_copy(h_bf[:], state[:])
                    v.drain().then_inc(hbc, 1)

    return nc



NIT = DIN // 128


def build_D():
    nc = bass.Bass("TRN2", target_bir_lowering=False)
    xs_d = nc.dram_tensor("xs", [TC, D], F32, kind="ExternalInput").ap()
    ytb_d = nc.dram_tensor("ytb", [DIN, TC], BF16, kind="ExternalInput").ap()
    ztb_d = nc.dram_tensor("ztb", [DIN, TC], BF16, kind="ExternalInput").ap()
    rmsw_d = nc.dram_tensor("rmsw", [128, NIT], F32, kind="ExternalInput").ap()
    woutT_d = nc.dram_tensor("woutT", [DIN, D], F32, kind="ExternalInput").ap()
    wguT_d = nc.dram_tensor("wguT", [D, 2 * INTER], F32,
                            kind="ExternalInput").ap()
    wdT_d = nc.dram_tensor("wdT", [INTER, D], F32, kind="ExternalInput").ap()
    id_d = nc.dram_tensor("identin", [128, 128], BF16,
                          kind="ExternalInput").ap()
    x4_d = nc.dram_tensor("x4_out", [TC, D], F32, kind="ExternalOutput").ap()

    with ExitStack() as ctx:
        sb = lambda n, s, d=F32: ctx.enter_context(nc.sbuf_tensor(n, s, d))
        ps = lambda n, s, d=F32: ctx.enter_context(nc.psum_tensor(n, s, d))
        sem = lambda n: ctx.enter_context(nc.semaphore(n))

        x_sb = sb("x_sb", [128, NT * D])
        ygr = sb("ygr", [128, NIT * TC], BF16)
        u2 = ygr[:, 0:NT * D]
        uT2 = ygr[:, NT * D:2 * NT * D]
        wbig = ygr[:, 0:8 * D]
        hT = sb("hT", [128, NDM * TC], BF16)
        wvq_sb = sb("wvq", [128, 2 * 8 * 512], BF16)
        wvq = [wvq_sb[:, 0:4096], wvq_sb[:, 4096:8192]]
        regC = sb("regC", [128, 6144], BF16)
        wcol = [regC[:, 0:2048], regC[:, 2048:4096]]
        hg = regC[:, 4096:6144]
        zb_sb = sb("zb", [128, 2 * TC], BF16)
        zb = [zb_sb[:, 0:TC], zb_sb[:, TC:2 * TC]]
        zsil = zb
        sq_sb = sb("sq", [128, 2 * TC])
        sq = [sq_sb[:, 0:TC], sq_sb[:, TC:2 * TC]]
        rstdB = sb("rstdB", [128, TC])
        rrow = sq_sb[0:1, 0:TC]
        rmsw_sb = sb("rmsw_sb", [128, NIT])
        ident = sb("ident", [128, 128], BF16)
        ones_colf = sb("ones_colf", [128, 1])
        onesf_row = sb("onesf_row", [1, 128])
        eps_col = sb("eps_col", [128, 1])
        raws = sb("raws", [128, NT]); m_cols = sb("m_cols", [128, NT])
        v_cols = sb("v_cols", [128, NT]); r_cols = sb("r_cols", [128, NT])
        s2_cols = sb("s2_cols", [128, NT])

        A = [ps(f"A{i}", [128, 512]) for i in range(4)]
        C2 = [ps(f"C{i}", [128, 512]) for i in range(4)]
        pool = A + C2

        ld = sem("ld"); yl = sem("yl"); zl = [sem("zl0"), sem("zl1")]
        gvv = sem("gvv"); szs = sem("szs"); sqm = sem("sqm")
        rsd = sem("rsd"); bcm = sem("bcm"); rsb = sem("rsb")
        ynd = sem("ynd"); wql = [sem("wql0"), sem("wql1")]
        wcn = sem("wcn"); wop = sem("wop"); woa = sem("woa")
        lnd = sem("lnd"); trs = sem("trs"); tre = sem("tre")
        gl = [sem("gl0"), sem("gl1")]; gf = sem("gf")
        fpr = sem("fpr"); fsi = sem("fsi"); fev = sem("fev")
        dl = [sem("dl0"), sem("dl1")]; dpr = sem("dpr"); dad = sem("dad")
        out_s = sem("out_s")

        block = ctx.enter_context(nc.Block())
        N_TR = NDM * NT // 4

        cfg = {
            "wguT_d": wguT_d, "wdT_d": wdT_d, "uT": uT2, "hT": hT,
            "wcol": wcol, "wbig": wbig, "hg": hg, "x_sb": x_sb, "A": A,
            "gl": gl, "gf": gf, "fpr": fpr, "fsi": fsi, "fev": fev,
            "dl": dl, "dpr": dpr, "dad": dad,
            "gf0": 0, "dad0": 0, "fev0": 0, "fpr0": 0, "fsi0": 0, "dpr0": 0,
            "gl0": [0, 0], "dl0": [0, 0],
        }

        @block.sync
        def _(sync):
            sync.dma_start(out=x_sb[:].rearrange("p (t d) -> p t d", t=NT),
                           in_=xs_d.rearrange("(t p) d -> p t d", p=128)
                           ).then_inc(ld, 16)
            sync.dma_start(out=ident[:], in_=id_d).then_inc(ld, 16)
            sync.dma_start(out=rmsw_sb[:], in_=rmsw_d).then_inc(ld, 16)
            sync.dma_start(out=ygr[:].rearrange("p (k c) -> p k c", k=NIT),
                           in_=ytb_d.rearrange("(k p) c -> p k c", p=128)
                           ).then_inc(yl, 16)
            for it in range(NIT):
                if it >= 2:
                    sync.wait_ge(gvv, it - 1)
                sync.dma_start(out=zb[it % 2],
                               in_=ztb_d[it * 128:(it + 1) * 128, :]
                               ).then_inc(zl[it % 2], 16)
            sync.wait_ge(out_s, 1)
            sync.dma_start(out=x4_d.rearrange("(t p) d -> p t d", p=128),
                           in_=x_sb[:].rearrange("p (t d) -> p t d", t=NT)
                           ).then_inc(ld, 16)
            sync.wait_ge(ld, 64)

        @block.gpsimd
        def _(g):
            qg = 0
            for nb in range(4):
                for kq in range(4):
                    slot = qg % 2
                    if qg >= 2:
                        g.wait_ge(wcn, qg - 1)
                    g.dma_start(
                        out=wvq[slot].rearrange("p (k c) -> p k c", k=8),
                        in_=woutT_d[kq * 1024:(kq + 1) * 1024,
                                    nb * 512:(nb + 1) * 512]
                            .rearrange("(k p) c -> p k c", p=128),
                    ).then_inc(wql[slot], 16)
                    qg += 1
            g.wait_ge(trs, N_TR)
            ffn_gpsimd(g, cfg)

        @block.tensor
        def _(t):
            t.wait_ge(ld, 32)
            for it in range(NIT):
                t.wait_ge(gvv, it + 1)
                s_t = sq[it % 2]
                t.matmul(A[0][0:1, 0:512], ones_colf[:], s_t[:, 0:512],
                         start=(it == 0), stop=(it == NIT - 1))
                t.matmul(A[1][0:1, 0:512], ones_colf[:], s_t[:, 512:1024],
                         start=(it == 0), stop=(it == NIT - 1))
                t.drain().then_inc(sqm, 1)
            t.wait_ge(rsd, 1)
            t.matmul(A[2][:, 0:512], onesf_row[:], rrow[0:1, 0:512],
                     start=True, stop=True)
            t.matmul(A[3][:, 0:512], onesf_row[:], rrow[0:1, 512:1024],
                     start=True, stop=True)
            t.drain().then_inc(bcm, 1)
            t.wait_ge(rsb, 1)
            qg = 0
            for nb in range(4):
                for kq in range(4):
                    slot = qg % 2
                    t.wait_ge(wql[slot], 16 * (qg // 2 + 1))
                    if nb == 0:
                        t.wait_ge(ynd, 8 * (kq + 1))
                    elif nb == 1 and kq == 0:
                        t.wait_ge(ynd, NIT)
                    for tt in range(8):
                        if kq == 0 and nb >= 1:
                            t.wait_ge(woa, 8 * (nb - 1) + tt + 1)
                        for ki in range(8):
                            kt = kq * 8 + ki
                            t.matmul(
                                pool[tt][:],
                                ygr[:, kt * TC + tt * 128:
                                    kt * TC + tt * 128 + 128],
                                wvq[slot][:, ki * 512:(ki + 1) * 512],
                                start=(kq == 0 and ki == 0),
                                stop=(kq == 3 and ki == 7))
                        if kq == 3:
                            t.drain().then_inc(wop, 1)
                    t.drain().then_inc(wcn, 1)
                    qg += 1
            t.wait_ge(lnd, 3)
            for r in range(N_TR):
                if r >= 2:
                    t.wait_ge(tre, r - 1)
                dmt, tt4 = divmod(r, NT // 4)
                for j in range(4):
                    tt = tt4 * 4 + j
                    t.matmul(A[r % 2][:, j * 128:(j + 1) * 128],
                             u2[:, tt * D + dmt * 128:tt * D + (dmt + 1) * 128],
                             ident[:], start=True, stop=True)
                t.drain().then_inc(trs, 1)
            t.wait_ge(tre, N_TR)
            ffn_tensor(t, cfg)

        @block.vector
        def _(v):
            v.memset(ones_colf[:], 1.0)
            v.memset(onesf_row[:], 1.0)
            v.memset(eps_col[:], EPS)
            v.drain()
            v.wait_ge(yl, 16)
            for it in range(NIT):
                v.wait_ge(szs, it + 1)
                if it >= 2:
                    v.wait_ge(sqm, it - 1)
                ysl = ygr[:, it * TC:(it + 1) * TC]
                v.tensor_mul(ysl, ysl, zsil[it % 2])
                v.drain()
                v.tensor_mul(sq[it % 2], ysl, ysl)
                v.drain().then_inc(gvv, 1)
            v.wait_ge(bcm, 1)
            v.tensor_copy(rstdB[:, 0:512], A[2][:, 0:512])
            v.tensor_copy(rstdB[:, 512:1024], A[3][:, 0:512])
            v.drain().then_inc(rsb, 1)
            for it in range(NIT):
                ysl = ygr[:, it * TC:(it + 1) * TC]
                v.scalar_tensor_tensor(out=ysl, in0=ysl,
                                       scalar=rmsw_sb[:, it:it + 1],
                                       in1=rstdB[:], op0=ALU.mult,
                                       op1=ALU.mult)
                v.drain().then_inc(ynd, 1)
            v.wait_ge(ld, 16)
            for r in range(32):
                v.wait_ge(wop, r + 1)
                nb, tt = divmod(r, 8)
                dst = x_sb[:, tt * D + nb * 512:tt * D + nb * 512 + 512]
                v.tensor_add(dst, dst, pool[tt][:])
                v.drain().then_inc(woa, 1)
            _ln_pre(v, x_sb, u2, NT, D, raws, m_cols, v_cols, s2_cols)
            v.drain().then_inc(lnd, 1)
            v.wait_ge(lnd, 2)
            _ln_post(v, x_sb, u2, NT, D, m_cols, r_cols)
            v.drain().then_inc(lnd, 1)
            for r in range(N_TR):
                v.wait_ge(trs, r + 1)
                dmt, tt4 = divmod(r, NT // 4)
                v.tensor_copy(
                    uT2[:, dmt * TC + tt4 * 512:dmt * TC + (tt4 + 1) * 512],
                    A[r % 2][:])
                v.drain().then_inc(tre, 1)
            ffn_vector(v, cfg)
            v.drain().then_inc(out_s, 1)

        @block.scalar
        def _(s):
            for it in range(NIT):
                s.wait_ge(zl[it % 2], 16 * (it // 2 + 1))
                if it >= 2:
                    s.wait_ge(gvv, it - 1)
                s.activation(zsil[it % 2], zb[it % 2], AF.Silu)
                s.drain().then_inc(szs, 1)
            s.wait_ge(sqm, NIT)
            s.activation(rrow[0:1, 0:512], A[0][0:1, 0:512], AF.Ln,
                         bias=eps_col[0:1, :], scale=1.0 / DIN)
            s.activation(rrow[0:1, 512:1024], A[1][0:1, 0:512], AF.Ln,
                         bias=eps_col[0:1, :], scale=1.0 / DIN)
            s.drain()
            s.activation(rrow[0:1, :], rrow[0:1, :], AF.Exp, scale=-0.5)
            s.drain().then_inc(rsd, 1)
            s.wait_ge(lnd, 1)
            _ln_mid(s, NT, v_cols, r_cols, eps_col[:])
            s.drain().then_inc(lnd, 1)
            ffn_scalar(s, cfg)

    return nc



def _rope_tables_f32():
    e = (np.arange(0, HD, 2).astype(np.float32) / np.float32(HD))
    base = np.power(np.float32(10000.0), e).astype(np.float32)
    inv = (np.float32(1.0) / base).astype(np.float32)
    tpos = np.arange(T, dtype=np.float32)
    ang = (tpos[:, None] * inv[None, :]).astype(np.float32)
    return np.ascontiguousarray(np.sin(ang).astype(np.float32).T), \
        np.ascontiguousarray(np.cos(ang).astype(np.float32).T)


_CACHE = {}


def _get(name, builder):
    if name not in _CACHE:
        _CACHE[name] = builder()
    return _CACHE[name]



def kernel(**inp):
    x = np.ascontiguousarray(np.asarray(inp["L0_attn_a_x"]).reshape(T, D))
    wq = np.asarray(inp["L0_attn_a_wq"])
    wk = np.asarray(inp["L0_attn_a_wk"])
    wv = np.asarray(inp["L0_attn_a_wv"])
    wo = np.asarray(inp["L0_attn_a_wo"])

    sinT, cosT = _rope_tables_f32()
    identity = np.eye(128, dtype=np.float32).astype(_bf)

    ncA = _get("A", build_A)
    wkT = np.ascontiguousarray(wk.T)
    wvT = np.ascontiguousarray(wv.T)
    in_maps = []
    for c in range(NCORES):
        sl = slice(c * TC, (c + 1) * TC)
        in_maps.append({
            "xs": np.ascontiguousarray(x[sl]),
            "wkT": wkT, "wvT": wvT,
            "cosT": np.ascontiguousarray(cosT[:, sl]),
            "sinT": np.ascontiguousarray(sinT[:, sl]),
            "identin": identity,
        })
    resA = run_bass_kernel_spmd(ncA, in_maps, core_ids=list(range(NCORES)))
    ktf = np.concatenate([resA.results[c]["kt_out"] for c in range(NCORES)],
                         axis=1)
    vtf = np.concatenate([resA.results[c]["vt_out"] for c in range(NCORES)],
                         axis=1)
    kn = np.stack([resA.results[c]["kn_out"][0] for c in range(NCORES)])
    kn8row = np.ascontiguousarray(kn.reshape(1, NCORES * NH))
    vhost = np.ascontiguousarray(
        vtf.T.reshape(64, 128, NH, 128).transpose(2, 1, 0, 3)
    ).reshape(NH, 128, T)

    wguT = np.ascontiguousarray(np.asarray(inp["L0_ffn_w_gate_up"]).T)
    wdT = np.ascontiguousarray(np.asarray(inp["L0_ffn_w_down"]).T)
    ncB = _get("B", build_B)
    wqT = np.ascontiguousarray(wq.T)
    woT = np.ascontiguousarray(wo.T)
    in_mapsB = []
    for c in range(NCORES):
        sl = slice(c * TC, (c + 1) * TC)
        in_mapsB.append({
            "xs": np.ascontiguousarray(x[sl]),
            "ktf": ktf, "vhost": vhost, "kn8row": kn8row,
            "wqT": wqT, "woT": woT, "wguT": wguT, "wdT": wdT,
            "cosT": np.ascontiguousarray(cosT[:, sl]),
            "sinT": np.ascontiguousarray(sinT[:, sl]),
            "identin": identity,
        })
    resB = run_bass_kernel_spmd(ncB, in_mapsB, core_ids=list(range(NCORES)))
    x2 = np.ascontiguousarray(np.concatenate(
        [resB.results[c]["x2_out"] for c in range(NCORES)], axis=0))

    W_inT = np.ascontiguousarray(np.asarray(inp["L0_attn_b_W_in"]).T)
    conv_w = np.asarray(inp["L0_attn_b_conv_w"])
    conv_b = np.asarray(inp["L0_attn_b_conv_b"])
    dt_bias = np.asarray(inp["L0_attn_b_dt_bias"])
    A_log = np.asarray(inp["L0_attn_b_A_log"])
    Dp = np.asarray(inp["L0_attn_b_Dp"])
    i256 = np.eye(256, dtype=np.float32).astype(_bf)
    maskneg = np.where(np.triu(np.ones((256, 256), bool)), 0.0,
                       -1e30).astype(np.float32)
    identf32 = np.eye(128, dtype=np.float32)
    ncC = _get("C", build_C)
    in_mapsC = []
    for c in range(NCORES):
        in_mapsC.append({
            "x2f": x2,
            "wxT": np.ascontiguousarray(
                W_inT[:, DIN + 512 * c:DIN + 512 * (c + 1)]),
            "wzT": np.ascontiguousarray(W_inT[:, 512 * c:512 * (c + 1)]),
            "wbcT": np.ascontiguousarray(W_inT[:, 2 * DIN:2 * DIN + 2 * DS]),
            "wdtT": np.ascontiguousarray(
                W_inT[:, 2 * DIN + 2 * DS + 8 * c:
                      2 * DIN + 2 * DS + 8 * c + 8]),
            "convX": np.ascontiguousarray(conv_w[512 * c:512 * (c + 1)]),
            "convBC": np.ascontiguousarray(conv_w[DIN:DIN + 2 * DS]),
            "cbX": np.ascontiguousarray(conv_b[512 * c:512 * (c + 1), None]),
            "cbBC": np.ascontiguousarray(conv_b[DIN:DIN + 2 * DS, None]),
            "dtb8": np.ascontiguousarray(dt_bias[8 * c:8 * c + 8, None]),
            "alog8": np.ascontiguousarray(A_log[8 * c:8 * c + 8, None]),
            "dp8": np.ascontiguousarray(Dp[8 * c:8 * c + 8, None]),
            "i256": i256, "maskneg": maskneg,
            "identin": identity, "identf32": identf32,
        })
    resC = run_bass_kernel_spmd(ncC, in_mapsC, core_ids=list(range(NCORES)))
    Y_T = np.concatenate([resC.results[c]["yt_out"] for c in range(NCORES)],
                         axis=0)
    Z_T = np.concatenate([resC.results[c]["zt_out"] for c in range(NCORES)],
                         axis=0)

    woutT = np.ascontiguousarray(np.asarray(inp["L0_attn_b_W_out"]).T)
    rmsw = np.ascontiguousarray(
        np.asarray(inp["L0_attn_b_norm_w"]).reshape(NIT, 128).T)
    ncD = _get("D", build_D)
    in_mapsD = []
    for c in range(NCORES):
        sl = slice(c * TC, (c + 1) * TC)
        in_mapsD.append({
            "xs": np.ascontiguousarray(x2[sl]),
            "ytb": np.ascontiguousarray(Y_T[:, sl]),
            "ztb": np.ascontiguousarray(Z_T[:, sl]),
            "rmsw": rmsw, "woutT": woutT, "wguT": wguT, "wdT": wdT,
            "identin": identity,
        })
    resD = run_bass_kernel_spmd(ncD, in_mapsD, core_ids=list(range(NCORES)))
    x4 = np.concatenate([resD.results[c]["x4_out"] for c in range(NCORES)],
                        axis=0).astype(np.float32)
    return x4.reshape(1, T, D)
